# version 05
# baseline (speedup 1.0000x reference)
import os
from contextlib import ExitStack

import numpy as np
import concourse.bass as bass
import concourse.mybir as mybir
from concourse.bass_utils import run_bass_kernel_spmd

F32 = mybir.dt.float32
BF16 = mybir.dt.bfloat16
AF = mybir.ActivationFunctionType
ALU = mybir.AluOpType
AX = mybir.AxisListType

D = 1024
S = 4096
NH = 16
NP = 8
E = 8
FF = 4096
NB = 16
VW = 5120
EPS = 1e-5

ENGINES = ("sync", "tensor", "vector", "scalar", "gpsimd")


def tpad(l):
    return 512 * ((l // 2) + 1)


class Sched:
    def __init__(self, nc):
        self.nc = nc
        self.ops = {e: [] for e in ENGINES}
        self.cnt = {}
        self.sems = {}

    def add_sem(self, name, handle):
        self.sems[name] = handle
        self.cnt[name] = 0

    def emit(self, eng, fn, inc=None, n=1):
        if inc is not None:
            self.cnt[inc] += n
        sem = self.sems[inc] if inc is not None else None

        def op(e, fn=fn, sem=sem, n=n):
            ins = fn(e)
            if sem is not None:
                ins.then_inc(sem, n)
        self.ops[eng].append(op)
        return (inc, self.cnt[inc]) if inc is not None else None

    def dma(self, eng, out, in_, inc="qs", **kw):
        return self.emit(eng, lambda e: e.dma_start(out=out, in_=in_, **kw),
                         inc=inc, n=16)

    def mark(self, sname):
        return (sname, self.cnt[sname])

    def wait(self, eng, mk):
        if not mk:
            return
        sname, val = mk
        if val <= 0:
            return
        sem = self.sems[sname]
        self.ops[eng].append(lambda e, sem=sem, val=val: e.wait_ge(sem, val))

    def build(self):
        with self.nc.Block() as block:
            for ename in ENGINES:
                ops = self.ops[ename]

                def body(e, ops=ops):
                    for op in ops:
                        op(e)
                getattr(block, ename)(body)


DBG = os.environ.get("KDBG", "")


def build_program():
    nc = bass.Bass("TRN2", target_bir_lowering=False)

    x_full = nc.dram_tensor("x_full", [S, D], F32, kind="ExternalInput")
    x_own = nc.dram_tensor("x_own", [2048, D], F32, kind="ExternalInput")
    wqkv = nc.dram_tensor("wqkv", [D, 3 * D], F32, kind="ExternalInput")
    wout = nc.dram_tensor("wout", [D, D], F32, kind="ExternalInput")
    rpey = nc.dram_tensor("rpey", [NH, VW], F32, kind="ExternalInput")
    maskv = nc.dram_tensor("maskv", [NH, VW], F32, kind="ExternalInput")
    gate = nc.dram_tensor("gate", [D, E], F32, kind="ExternalInput")
    w1 = nc.dram_tensor("w1", [E, D, FF], F32, kind="ExternalInput")
    w3 = nc.dram_tensor("w3", [E, D, FF], F32, kind="ExternalInput")
    w2 = nc.dram_tensor("w2", [E, FF, D], F32, kind="ExternalInput")
    ident = nc.dram_tensor("ident", [128, 128], F32, kind="ExternalInput")
    y_out = nc.dram_tensor("y", [2048, D], F32, kind="ExternalOutput")

    k_scr = nc.dram_tensor("k_scr", [NP * 128, S], BF16, kind="Internal")
    q_scr = nc.dram_tensor("q_scr", [NP * 128, 2048], BF16, kind="Internal")
    vf_scr = nc.dram_tensor("vf_scr", [NH, VW], F32, kind="Internal")
    y1_scr = nc.dram_tensor("y1_scr", [2048, D], F32, kind="Internal")

    glob = ExitStack()

    def gsb(name, shape, dt):
        return glob.enter_context(nc.sbuf_tensor(name, shape, dt))

    S_ = Sched(nc)
    for s in ("qs", "qg", "pe", "dv", "ac", "so"):
        S_.add_sem(s, glob.enter_context(nc.semaphore(s)))

    mk = {}

    ident_bf = gsb("ident_bf", [128, 128], BF16)
    mx = gsb("mx", [128, 16], F32)
    mneg = [gsb(f"mneg{i}", [128, 1], F32) for i in range(2)]
    lsum = gsb("lsum", [128, 1], F32)
    rsc = gsb("rsc", [128, 1], F32)
    ln_s = gsb("ln_s", [128, 1], F32)
    ln_ss = gsb("ln_ss", [128, 1], F32)
    ln_t1 = gsb("ln_t1", [128, 1], F32)
    ln_t2 = gsb("ln_t2", [128, 1], F32)
    ln_r = gsb("ln_r", [128, 1], F32)
    ln_b = gsb("ln_b", [128, 1], F32)

    pA0 = ExitStack()
    v_sb = pA0.enter_context(nc.sbuf_tensor("v_sb", [128, 32 * D], BF16))
    mix_fm = [pA0.enter_context(nc.sbuf_tensor(f"mix_fm{i}", [128, NP * 128], BF16))
              for i in range(2)]
    wout_sb = pA0.enter_context(nc.sbuf_tensor("wout_sb", [128, NP * D], BF16))

    pv = ExitStack()
    vf_sb = pv.enter_context(nc.sbuf_tensor("vf_sb", [NH, VW], F32))
    rpey_sb = pv.enter_context(nc.sbuf_tensor("rpey_sb", [NH, VW], F32))
    maskv_sb = pv.enter_context(nc.sbuf_tensor("maskv_sb", [NH, VW], F32))
    zeros_v = pv.enter_context(nc.sbuf_tensor("zeros_v", [NH, VW], F32))

    S_.dma("gpsimd", ident_bf[:, :], ident[:, :], "qg")
    mk["ident"] = S_.mark("qg")
    src = bass.AP(wout, 0, [[D, 128], [128 * D, 8], [1, D]])
    S_.dma("gpsimd", wout_sb[:, :].rearrange("p (c f) -> p c f", c=8), src, "qg")
    S_.dma("sync", rpey_sb[:, :], rpey[:, :], "qs")
    S_.dma("sync", maskv_sb[:, :], maskv[:, :], "qs")
    mk["rpe"] = S_.mark("qs")
    S_.wait("vector", mk["rpe"])
    S_.emit("vector", lambda e: e.memset(zeros_v[:, :], 0.0), inc="dv")
    S_.wait("vector", S_.mark("dv"))
    S_.emit("vector", lambda e: e.tensor_tensor_scan(
        vf_sb[:, :], rpey_sb[:, :], zeros_v[:, :], 0.0, ALU.add, ALU.add), inc="dv")
    S_.wait("vector", S_.mark("dv"))
    S_.emit("vector", lambda e: e.tensor_add(
        vf_sb[:, :], vf_sb[:, :], maskv_sb[:, :]), inc="dv")
    S_.wait("sync", S_.mark("dv"))
    S_.dma("sync", vf_scr[:, :], vf_sb[:, :], "qs")
    mk["vfdone"] = S_.mark("qs")
    pv.close()

    p0 = ExitStack()

    def sb0(name, shape, dt):
        return p0.enter_context(nc.sbuf_tensor(name, shape, dt))

    def ps0(name, shape, dt):
        return p0.enter_context(nc.psum_tensor(name, shape, dt))

    wqkv_sb = sb0("wqkv_sb", [128, 8 * 3072], BF16)
    xstage = [sb0(f"xstage{i}", [128, D], BF16) for i in range(8)]
    xfm_win = [sb0(f"xfm_win{i}", [128, 8 * 512], BF16) for i in range(2)]
    kstage = [sb0(f"kstage{i}", [128, 512], BF16) for i in range(2)]
    tr_ps = [ps0(f"tr_ps{i}", [128, 1024], BF16) for i in range(2)]
    mm_ps = [ps0(f"mm_ps{i}", [128, 512], F32) for i in range(4)]

    S_.wait("gpsimd", mk["vfdone"])
    src = bass.AP(wqkv, 0, [[3072, 128], [128 * 3072, 8], [1, 3072]])
    S_.dma("gpsimd", wqkv_sb[:, :].rearrange("p (c f) -> p c f", c=8), src, "qg")
    mk["wqkv"] = S_.mark("qg")

    def emit_xload(widx, tile, which):
        slot = (widx % 2) * 4 + tile
        srcT = x_full if which == 0 else x_own
        row0 = widx * 512 + tile * 128
        S_.wait("gpsimd", mk.get(("xstage", slot)))
        S_.dma("gpsimd", xstage[slot][:, :], srcT[row0:row0 + 128, :], "qg")
        mk[("xload", widx % 2, tile)] = S_.mark("qg")

    def emit_window(widx):
        win = xfm_win[widx % 2]
        for tile in range(4):
            slot = (widx % 2) * 4 + tile
            bank = tr_ps[tile % 2]
            S_.wait("tensor", mk[("xload", widx % 2, tile)])
            S_.wait("tensor", mk["ident"])
            S_.wait("tensor", mk.get(("tr_ps", tile % 2)))
            for c in range(8):
                S_.emit("tensor", lambda e, b=bank, s=xstage[slot], c=c: e.transpose(
                    b[:, c * 128:(c + 1) * 128], s[:, c * 128:(c + 1) * 128],
                    ident_bf[:, :]), inc="pe")
            mk[("xstage", slot)] = S_.mark("pe")
            S_.wait("vector", S_.mark("pe"))
            dst = bass.AP(win, tile * 128, [[4096, 128], [512, 8], [1, 128]])
            S_.emit("vector", lambda e, b=bank, d=dst: e.tensor_copy(
                d, b[:, :].rearrange("p (c f) -> p c f", c=8)), inc="dv")
            mk[("tr_ps", tile % 2)] = S_.mark("dv")
        mk[("win", widx % 2)] = S_.mark("dv")

    def emit_proj(widx, kind):
        win = xfm_win[widx % 2]
        coff = 1024 if kind == "k" else 0
        scr = k_scr if kind == "k" else q_scr
        for pair in range(NP):
            bank = mm_ps[pair % 2]
            S_.wait("tensor", mk["wqkv"])
            S_.wait("tensor", mk[("win", widx % 2)])
            S_.wait("tensor", mk.get(("mm_ps", pair % 2)))
            for c in range(8):
                S_.emit("tensor", lambda e, b=bank, c=c, p=pair, w=win, co=coff:
                        e.matmul(
                    b[:, :],
                    wqkv_sb[:, c * 3072 + co + p * 128:c * 3072 + co + (p + 1) * 128],
                    w[:, c * 512:(c + 1) * 512],
                    start=(c == 0), stop=(c == 7), skip_group_check=True), inc="pe")
            tpe = S_.mark("pe")
            slot = kstage[pair % 2]
            S_.wait("vector", tpe)
            S_.wait("vector", mk.get(("kstage", pair % 2)))
            if kind == "q":
                S_.emit("vector", lambda e, b=bank, s=slot: e.tensor_scalar(
                    s[:, :], b[:, :], 0.125, None, ALU.mult), inc="dv")
            else:
                S_.emit("vector", lambda e, b=bank, s=slot: e.tensor_copy(
                    s[:, :], b[:, :]), inc="dv")
            mk[("mm_ps", pair % 2)] = S_.mark("dv")
            S_.wait("sync", S_.mark("dv"))
            S_.dma("sync", scr[pair * 128:(pair + 1) * 128,
                               widx * 512:(widx + 1) * 512], slot[:, :], "qs")
            mk[("kstage", pair % 2)] = S_.mark("qs")

    def emit_v(widx):
        win = xfm_win[widx % 2]
        for blk in range(4):
            for half in range(2):
                bank = mm_ps[2 + half]
                S_.wait("tensor", mk[("win", widx % 2)])
                S_.wait("tensor", mk.get(("mm_ps", 2 + half)))
                for c in range(8):
                    S_.emit("tensor", lambda e, b=bank, c=c, bl=blk, h=half, w=win:
                            e.matmul(
                        b[:, :], w[:, c * 512 + bl * 128:c * 512 + (bl + 1) * 128],
                        wqkv_sb[:, c * 3072 + 2048 + h * 512:
                                c * 3072 + 2048 + (h + 1) * 512],
                        start=(c == 0), stop=(c == 7), skip_group_check=True),
                        inc="pe")
                S_.wait("vector", S_.mark("pe"))
                tt = widx * 4 + blk
                S_.emit("vector", lambda e, b=bank, tt=tt, h=half: e.tensor_copy(
                    v_sb[:, tt * D + h * 512:tt * D + (h + 1) * 512], b[:, :]),
                    inc="dv")
                mk[("mm_ps", 2 + half)] = S_.mark("dv")

    for tile in range(4):
        emit_xload(0, tile, 0)
    for w in range(8):
        if w + 1 < 8:
            for tile in range(4):
                emit_xload(w + 1, tile, 0)
        emit_window(w)
        emit_proj(w, "k")
        emit_v(w)
    for w in range(4):
        for tile in range(4):
            emit_xload(w, tile, 1)
        emit_window(w)
        emit_proj(w, "q")

    join0 = [S_.mark("pe"), S_.mark("dv"), S_.mark("qs"), S_.mark("qg")]
    for ename in ENGINES:
        for m in join0:
            S_.wait(ename, m)
    p0.close()

    if DBG == "p0":
        S_.dma("gpsimd", bass.AP(y_out, 0, [[1024, 80], [1, 1024]]),
               bass.AP(vf_scr, 0, [[1024, 80], [1, 1024]]), "so")
        S_.dma("gpsimd", bass.AP(y_out, 128 * 1024, [[1024, 512], [1, 1024]]),
               bass.AP(k_scr, 0, [[4096, 512], [1, 1024]]), "so")
        S_.dma("gpsimd", bass.AP(y_out, 640 * 1024, [[1024, 512], [1, 1024]]),
               bass.AP(q_scr, 0, [[2048, 512], [1, 1024]]), "so")
        for i in range(4):
            S_.dma("gpsimd",
                   bass.AP(y_out, (1152 + i * 128) * 1024, [[1024, 128], [1, 1024]]),
                   v_sb[:, i * 8192:i * 8192 + 1024], "so")
        S_.wait("gpsimd", S_.mark("so"))
        S_.build()
        pA0.close()
        glob.close()
        return nc

    pa = ExitStack()

    def sba(name, shape, dt):
        return pa.enter_context(nc.sbuf_tensor(name, shape, dt))

    def psa(name, shape, dt):
        return pa.enter_context(nc.psum_tensor(name, shape, dt))

    qwin_t = sba("qwin", [64, NH * 128], BF16)
    qwin = [qwin_t, qwin_t]
    kwin = [sba(f"kwin{i}", [64, S], BF16) for i in range(2)]
    patch = [sba(f"patch{i}", [128, S], F32) for i in range(2)]
    S_sb = [sba(f"S_sb{i}", [128, S], F32) for i in range(2)]
    P_sb = [sba(f"P_sb{i}", [128, S], BF16) for i in range(2)]
    PT_sb = [sba(f"PT_sb{i}", [128, 1024], BF16) for i in range(2)]
    xres_t = sba("xres", [128, D], F32)
    xres = [xres_t, xres_t]
    z_sb = xres_t
    y1f = xres_t

    sbank = [psa(f"sbank{i}", [128, 512], F32) for i in range(2)]
    ptb = [psa(f"ptb{i}", [128, 1024], BF16) for i in range(2)]
    mixb = [psa(f"mixb{i}", [128, 512], F32) for i in range(2)]
    ob = [psa(f"ob{i}", [128, 512], F32) for i in range(2)]

    def emit_qx_load(l):
        S_.wait("sync", mk.get(("qwin",)))
        src = bass.AP(q_scr, l * 128, [[2048, 64], [64 * 2048, 16], [1, 128]])
        S_.dma("sync", qwin[l % 2][:, :].rearrange("p (a f) -> p a f", a=16),
               src, "qs")
        mk[("qwin_ld",)] = S_.mark("qs")
        S_.wait("sync", mk.get(("xres",)))
        S_.dma("sync", xres[l % 2][:, :], x_own[l * 128:(l + 1) * 128, :], "qs")
        mk[("xres_ld",)] = S_.mark("qs")

    def emit_kp_load(l, h):
        slot = h % 2
        tp = tpad(l)
        pair, hh = h // 2, h % 2
        S_.wait("sync", mk.get(("kwin", slot)))
        src = bass.AP(k_scr, (pair * 128 + hh * 64) * S, [[S, 64], [1, tp]])
        S_.dma("sync", kwin[slot][:, 0:tp], src, "qs")
        mk[("kwin_ld", slot)] = S_.mark("qs")
        S_.wait("sync", mk.get(("patch", slot)))
        o0 = 1025 + 256 * l - tp
        src = bass.AP(vf_scr, h * VW + o0, [[2, 128], [1, tp]])
        S_.dma("sync", patch[slot][:, 0:tp], src, "qs")
        mk[("patch_ld", slot)] = S_.mark("qs")

    def emit_scores(l, h):
        tp = tpad(l)
        ntc = tp // 512
        slot = h % 2
        kload = mk[("kwin_ld", slot)]
        pload = mk[("patch_ld", slot)]
        qload = mk[("qwin_ld",)]
        for tc in range(ntc):
            bank = sbank[tc % 2]
            S_.wait("tensor", kload)
            S_.wait("tensor", qload)
            S_.wait("tensor", mk.get(("sbank", tc % 2)))
            S_.emit("tensor", lambda e, b=bank, tc=tc, h=h, sl=slot,
                    lp=l % 2: e.matmul(
                b[:, :],
                qwin[lp][0:64, h * 128:(h + 1) * 128],
                kwin[sl][0:64, tc * 512:(tc + 1) * 512],
                start=True, stop=True, skip_group_check=True), inc="pe")
            tpe = S_.mark("pe")
            S_.wait("vector", tpe)
            S_.wait("vector", pload)
            S_.wait("vector", mk.get(("S_free", h % 2)))
            rev = bass.AP(patch[slot], (tp - 1) - tc * 512, [[S, 128], [-1, 512]])
            S_.emit("vector", lambda e, b=bank, tc=tc, r=rev, hp=h % 2: e.tensor_add(
                S_sb[hp][:, tc * 512:(tc + 1) * 512], b[:, :], r), inc="dv")
            mk[("sbank", tc % 2)] = S_.mark("dv")
        mk[("kwin", slot)] = S_.mark("pe")
        mk[("patch", slot)] = S_.mark("dv")
        S_.wait("vector", S_.mark("dv"))
        S_.emit("vector", lambda e, tp=tp, hp=h % 2: e.reduce_max(
            mneg[hp][:, :], S_sb[hp][:, 0:tp], axis=AX.X, negate=True), inc="dv")
        mk[("stats", h % 2)] = S_.mark("dv")

    def emit_soft(l, h):
        tp = tpad(l)
        hp = h % 2
        S_.wait("scalar", mk[("stats", hp)])
        S_.wait("scalar", mk.get(("P_free", hp)))
        S_.wait("scalar", mk.get("rsc_free"))
        S_.emit("scalar", lambda e, tp=tp, hp=hp: e.activation(
            P_sb[hp][:, 0:tp], S_sb[hp][:, 0:tp], AF.Exp, bias=mneg[hp][:, :],
            scale=1.0, accum_out=lsum[:, :]), inc="ac")
        mk[("S_free", hp)] = S_.mark("ac")
        S_.wait("scalar", S_.mark("ac"))
        S_.emit("scalar", lambda e: e.activation(rsc[:, :], lsum[:, :], AF.Ln),
                inc="ac")
        S_.wait("scalar", S_.mark("ac"))
        S_.emit("scalar", lambda e: e.activation(
            rsc[:, :], rsc[:, :], AF.Exp, scale=-1.0), inc="ac")
        t_r = S_.mark("ac")
        S_.wait("vector", t_r)
        S_.emit("vector", lambda e, tp=tp, hp=hp: e.tensor_scalar(
            P_sb[hp][:, 0:tp], P_sb[hp][:, 0:tp], rsc[:, :], None, ALU.mult),
            inc="dv")
        mk["rsc_free"] = S_.mark("dv")
        mk[("pscale", hp)] = S_.mark("dv")

    def emit_mix(l, h):
        tp = tpad(l)
        slot = h % 2
        pair, hh = h // 2, h % 2
        hp = h % 2
        t_ps = mk[("pscale", hp)]
        ntc128 = tp // 128
        mixbank = mixb[pair // 4]
        mm_i = 0
        for g0 in range(0, ntc128, 8):
            gn = min(8, ntc128 - g0)
            bslot = (g0 // 8) % 2
            bank = ptb[bslot]
            S_.wait("tensor", t_ps)
            S_.wait("tensor", mk.get(("ptb", bslot)))
            for j in range(gn):
                S_.emit("tensor", lambda e, b=bank, j=j, g0=g0, hp=hp: e.transpose(
                    b[:, j * 128:(j + 1) * 128],
                    P_sb[hp][:, (g0 + j) * 128:(g0 + j + 1) * 128],
                    ident_bf[:, :]), inc="pe")
            tpe = S_.mark("pe")
            eng, sem = (("vector", "dv") if bslot == 0 else ("scalar", "ac"))
            S_.wait(eng, tpe)
            S_.wait(eng, mk.get(("PT", bslot)))
            if eng == "vector":
                S_.emit(eng, lambda e, b=bank, sl=bslot, gn=gn: e.tensor_copy(
                    PT_sb[sl][:, 0:gn * 128], b[:, 0:gn * 128]), inc=sem)
            else:
                S_.emit(eng, lambda e, b=bank, sl=bslot, gn=gn: e.copy(
                    PT_sb[sl][:, 0:gn * 128], b[:, 0:gn * 128]), inc=sem)
            mk[("ptb", bslot)] = S_.mark(sem)
            S_.wait("tensor", S_.mark(sem))
            if g0 == 0:
                S_.wait("tensor", mk.get(("mixb", pair // 4)))
            for j in range(gn):
                tc = g0 + j
                st_flag = (mm_i == 0)
                S_.emit("tensor", lambda e, j=j, tc=tc, sl=bslot, st=st_flag,
                        p=pair, hh=hh, mb=mixbank: e.matmul(
                    mb[hh * 64:hh * 64 + 64, (p % 4) * 128:(p % 4 + 1) * 128],
                    v_sb[:, tc * D + (p * 2 + hh) * 64:
                         tc * D + (p * 2 + hh) * 64 + 64],
                    PT_sb[sl][:, j * 128:(j + 1) * 128],
                    start=st, stop=False, skip_group_check=True), inc="pe")
                mm_i += 1
            mk[("PT", bslot)] = S_.mark("pe")
        mk[("P_free", hp)] = S_.mark("pe")
        mk[("mix_done", pair // 4)] = S_.mark("pe")

    def emit_attn_tail(l):
        for bk in range(2):
            S_.wait("vector", mk[("mix_done", bk)])
            S_.wait("vector", mk.get(("mixfm", l % 2)))
            S_.emit("vector", lambda e, bk=bk, m=mix_fm[l % 2]: e.tensor_copy(
                m[:, bk * 512:(bk + 1) * 512], mixb[bk][:, :]), inc="dv")
            mk[("mixb", bk)] = S_.mark("dv")
        t_mix = S_.mark("dv")
        for half in range(2):
            bank = ob[half]
            S_.wait("tensor", t_mix)
            S_.wait("tensor", mk.get(("ob", half)))
            for p in range(NP):
                S_.emit("tensor", lambda e, b=bank, p=p, hf=half,
                        m=mix_fm[l % 2]: e.matmul(
                    b[:, :], m[:, p * 128:(p + 1) * 128],
                    wout_sb[:, p * D + hf * 512:p * D + (hf + 1) * 512],
                    start=(p == 0), stop=(p == 7), skip_group_check=True), inc="pe")
        t_op = S_.mark("pe")
        mk[("mixfm", l % 2)] = S_.mark("pe")
        S_.wait("vector", t_op)
        S_.wait("vector", mk[("xres_ld",)])
        for half in range(2):
            S_.emit("vector", lambda e, hf=half, xr=xres[l % 2]: e.tensor_add(
                z_sb[:, hf * 512:(hf + 1) * 512], ob[hf][:, :],
                xr[:, hf * 512:(hf + 1) * 512]), inc="dv")
        mk[("ob", 0)] = S_.mark("dv")
        mk[("ob", 1)] = S_.mark("dv")
        S_.wait("vector", S_.mark("dv"))
        emit_ln(z_sb, y1f, ("y1f",))
        mk[("xres",)] = S_.mark("dv")
        S_.wait("sync", S_.mark("dv"))
        S_.dma("sync", y1_scr[l * 128:(l + 1) * 128, :], y1f[:, :], "qs")
        mk[("y1f",)] = S_.mark("qs")

    def emit_ln(zt, yt, freekey, sem_free="qs"):
        S_.emit("vector", lambda e: e.reduce_sum(
            ln_s[:, :], zt[:, :], axis=AX.X, negate=True), inc="dv")
        S_.wait("vector", S_.mark("dv"))
        S_.emit("vector", lambda e: e.tensor_scalar(
            ln_s[:, :], ln_s[:, :], 1.0 / D, None, ALU.mult), inc="dv")
        t_m = S_.mark("dv")
        S_.wait("scalar", t_m)
        S_.wait("scalar", mk.get(("P_free", 1)))
        S_.emit("scalar", lambda e: e.activation(
            P_sb[1][:, 0:D], zt[:, :], AF.Square, accum_out=ln_ss[:, :]), inc="ac")
        t_sq = S_.mark("ac")
        S_.wait("vector", t_sq)
        S_.emit("vector", lambda e: e.tensor_scalar(
            ln_t1[:, :], ln_ss[:, :], 1.0 / D, None, ALU.mult), inc="dv")
        S_.emit("vector", lambda e: e.tensor_mul(
            ln_t2[:, :], ln_s[:, :], ln_s[:, :]), inc="dv")
        S_.wait("vector", S_.mark("dv"))
        S_.emit("vector", lambda e: e.tensor_sub(
            ln_t1[:, :], ln_t1[:, :], ln_t2[:, :]), inc="dv")
        S_.wait("vector", S_.mark("dv"))
        S_.emit("vector", lambda e: e.tensor_scalar(
            ln_t1[:, :], ln_t1[:, :], EPS, None, ALU.add), inc="dv")
        t_v = S_.mark("dv")
        S_.wait("scalar", t_v)
        S_.emit("scalar", lambda e: e.activation(ln_r[:, :], ln_t1[:, :], AF.Ln),
                inc="ac")
        S_.wait("scalar", S_.mark("ac"))
        S_.emit("scalar", lambda e: e.activation(
            ln_r[:, :], ln_r[:, :], AF.Exp, scale=-0.5), inc="ac")
        t_rs = S_.mark("ac")
        S_.wait("vector", t_rs)
        S_.emit("vector", lambda e: e.tensor_scalar(
            yt[:, :], zt[:, :], ln_s[:, :], ln_r[:, :], ALU.add, ALU.mult),
            inc="dv")

    NBeff = NB
    NHeff = NH
    if DBG.startswith("attn"):
        parts = DBG[4:].rstrip("z").split("h")
        NBeff = int(parts[0]) if parts[0] else NB
        NHeff = int(parts[1]) if len(parts) > 1 else NH
    emit_qx_load(0)
    emit_kp_load(0, 0)
    for l in range(NBeff):
        emit_scores(l, 0)
        for h in range(NHeff):
            emit_soft(l, h)
            if h + 1 < NHeff:
                emit_kp_load(l, h + 1)
                emit_scores(l, h + 1)
            elif l + 1 < NBeff:
                emit_kp_load(l + 1, 0)
            emit_mix(l, h)
        if NHeff == NH:
            emit_attn_tail(l)
        mk[("qwin",)] = S_.mark("pe")
        if l + 1 < NBeff:
            emit_qx_load(l + 1)

    joinA = [S_.mark("pe"), S_.mark("dv"), S_.mark("ac"), S_.mark("qs")]
    for ename in ENGINES:
        for m in joinA:
            S_.wait(ename, m)

    if DBG.startswith("attn"):
        if NHeff == NH:
            S_.dma("gpsimd", y_out[0:NBeff * 128, :], y1_scr[0:NBeff * 128, :], "so")
            pass
        else:
            S_.dma("gpsimd", y_out[0:128, :], S_sb[0][:, 0:1024], "so")
            S_.dma("gpsimd", y_out[128:256, :], P_sb[0][:, 0:1024], "so")
            S_.dma("gpsimd", y_out[256:384, :], mix_fm[0][:, 0:1024], "so")
        S_.wait("gpsimd", S_.mark("so"))
        S_.build()
        pa.close()
        pA0.close()
        glob.close()
        return nc
    pa.close()
    pA0.close()

    pf = ExitStack()

    def sbf(name, shape, dt):
        return pf.enter_context(nc.sbuf_tensor(name, shape, dt))

    y1_fm = sbf("y1_fm", [128, 8 * 2048], BF16)
    acc = sbf("acc", [128, 8 * D], F32)
    hT = sbf("hT", [128, 32 * 1024], BF16)
    w1b = [sbf(f"w1b{i}", [128, 8 * 512], BF16) for i in range(2)]
    w3b = [sbf(f"w3b{i}", [128, 8 * 512], BF16) for i in range(2)]
    w2b = [sbf(f"w2b{i}", [128, D], BF16) for i in range(4)]
    gate_sb = sbf("gate_sb", [128, 8 * E], BF16)
    wi_sb = sbf("wi_sb", [128, NB * E], F32)
    lg_sb = sbf("lg_sb", [128, E], F32)
    eq1 = sbf("eq1", [128, E], F32)
    eq2 = sbf("eq2", [128, E], F32)
    gm1 = sbf("gm1", [128, 1], F32)
    gm2 = sbf("gm2", [128, 1], F32)
    gs1 = sbf("gs1", [128, 1], F32)
    gs2 = sbf("gs2", [128, 1], F32)
    sG = [sbf(f"sG{i}", [128, 512], F32) for i in range(2)]
    ystage = [sbf(f"ystage{i}", [128, D], F32) for i in range(2)]
    y1stage = [sbf(f"y1stage{i}", [128, D], BF16) for i in range(2)]
    junkf = sbf("junkf", [128, D], F32)

    pf1 = ExitStack()
    trb = pf1.enter_context(nc.psum_tensor("trb", [128, 1024], BF16))
    glog = pf1.enter_context(nc.psum_tensor("glog", [128, 512], F32))

    src = bass.AP(gate, 0, [[E, 128], [128 * E, 8], [1, E]])
    S_.dma("gpsimd", gate_sb[:, :].rearrange("p (c f) -> p c f", c=8), src, "qg")
    mk["gate"] = S_.mark("qg")

    for l in range(NB):
        slot = l % 2
        S_.wait("gpsimd", mk.get(("y1stage", slot)))
        S_.dma("gpsimd", y1stage[slot][:, :], y1_scr[l * 128:(l + 1) * 128, :], "qg")
        t_ld = S_.mark("qg")
        S_.wait("tensor", t_ld)
        S_.wait("tensor", mk.get("trb"))
        for c in range(8):
            S_.emit("tensor", lambda e, c=c, sl=slot: e.transpose(
                trb[:, c * 128:(c + 1) * 128],
                y1stage[sl][:, c * 128:(c + 1) * 128], ident_bf[:, :]), inc="pe")
        mk[("y1stage", slot)] = S_.mark("pe")
        S_.wait("vector", S_.mark("pe"))
        dst = bass.AP(y1_fm, l * 128, [[8 * 2048, 128], [2048, 8], [1, 128]])
        S_.emit("vector", lambda e, d=dst: e.tensor_copy(
            d, trb[:, :].rearrange("p (c f) -> p c f", c=8)), inc="dv")
        mk["trb"] = S_.mark("dv")
    t_y1fm = S_.mark("dv")

    for l in range(NB):
        S_.wait("tensor", mk["gate"])
        S_.wait("tensor", t_y1fm)
        S_.wait("tensor", mk.get("glog"))
        for c in range(8):
            S_.emit("tensor", lambda e, c=c, l=l: e.matmul(
                glog[:, 0:E], y1_fm[:, c * 2048 + l * 128:c * 2048 + (l + 1) * 128],
                gate_sb[:, c * E:(c + 1) * E],
                start=(c == 0), stop=(c == 7), skip_group_check=True), inc="pe")
        S_.wait("vector", S_.mark("pe"))
        S_.emit("vector", lambda e: e.tensor_copy(lg_sb[:, :], glog[:, 0:E]),
                inc="dv")
        mk["glog"] = S_.mark("dv")
        S_.wait("vector", S_.mark("dv"))
        S_.emit("vector", lambda e: e.reduce_max(
            gm1[:, :], lg_sb[:, :], axis=AX.X), inc="dv")
        S_.wait("vector", S_.mark("dv"))
        S_.emit("vector", lambda e: e.tensor_scalar(
            eq1[:, :], lg_sb[:, :], gm1[:, :], None, ALU.is_ge), inc="dv")
        S_.wait("vector", S_.mark("dv"))
        S_.emit("vector", lambda e: e.scalar_tensor_tensor(
            eq2[:, :], eq1[:, :], -1e30, lg_sb[:, :], ALU.mult, ALU.add), inc="dv")
        S_.wait("vector", S_.mark("dv"))
        S_.emit("vector", lambda e: e.reduce_max(
            gm2[:, :], eq2[:, :], axis=AX.X), inc="dv")
        S_.wait("vector", S_.mark("dv"))
        S_.emit("vector", lambda e: e.tensor_scalar(
            eq2[:, :], eq2[:, :], gm2[:, :], None, ALU.is_ge), inc="dv")
        S_.emit("vector", lambda e: e.tensor_sub(
            gm2[:, :], gm2[:, :], gm1[:, :]), inc="dv")
        S_.wait("scalar", S_.mark("dv"))
        S_.emit("scalar", lambda e: e.activation(
            gs2[:, :], gm2[:, :], AF.Sigmoid), inc="ac")
        S_.wait("vector", S_.mark("ac"))
        S_.emit("vector", lambda e: e.tensor_scalar(
            gs1[:, :], gs2[:, :], -1.0, 1.0, ALU.mult, ALU.add), inc="dv")
        S_.wait("vector", S_.mark("dv"))
        S_.emit("vector", lambda e, l=l: e.tensor_scalar(
            wi_sb[:, l * E:(l + 1) * E], eq1[:, :], gs1[:, :], None, ALU.mult),
            inc="dv")
        S_.wait("vector", S_.mark("dv"))
        S_.emit("vector", lambda e, l=l: e.scalar_tensor_tensor(
            wi_sb[:, l * E:(l + 1) * E], eq2[:, :], gs2[:, :],
            wi_sb[:, l * E:(l + 1) * E], ALU.mult, ALU.add), inc="dv")
    t_wi = S_.mark("dv")

    if DBG == "ffng":
        for m in [S_.mark("pe"), S_.mark("dv"), S_.mark("ac")]:
            S_.wait("gpsimd", m)
        S_.dma("gpsimd", bass.AP(y_out, 0, [[1024, 128], [1, 128]]),
               wi_sb[:, 0:128], "so")
        S_.dma("gpsimd", bass.AP(y_out, 384 * 1024, [[1024, 128], [1, 8]]),
               lg_sb[:, :], "so")
        S_.dma("gpsimd", bass.AP(y_out, 1664 * 1024, [[1024, 128], [1, 1024]]),
               y1_scr[1920 * 1024], "so") if False else None
        S_.dma("gpsimd", y_out[1664:1792, :], y1_scr[1920:2048, :], "so")
        S_.dma("gpsimd", bass.AP(y_out, 400 * 1024 + 0, [[1024, 128], [1, 1]]),
               gm1[:, :], "so", allow_slow_non_contiguous=True)
        S_.dma("gpsimd", bass.AP(y_out, 400 * 1024 + 1, [[1024, 128], [1, 1]]),
               gm2[:, :], "so", allow_slow_non_contiguous=True)
        S_.dma("gpsimd", bass.AP(y_out, 400 * 1024 + 2, [[1024, 128], [1, 1]]),
               gs1[:, :], "so", allow_slow_non_contiguous=True)
        S_.dma("gpsimd", bass.AP(y_out, 400 * 1024 + 3, [[1024, 128], [1, 1]]),
               gs2[:, :], "so", allow_slow_non_contiguous=True)
        S_.dma("gpsimd", bass.AP(y_out, 400 * 1024 + 8, [[1024, 128], [1, 8]]),
               eq1[:, :], "so")
        S_.dma("gpsimd", bass.AP(y_out, 400 * 1024 + 16, [[1024, 128], [1, 8]]),
               eq2[:, :], "so")
        S_.dma("gpsimd", y_out[256:384, :], y1_fm[:, 0:1024], "so")
        S_.dma("gpsimd", y_out[512:1536, :], y1_scr[0:1024, :], "so")
        S_.wait("gpsimd", S_.mark("so"))
        S_.build()
        pf1.close()
        pf.close()
        glob.close()
        return nc

    joinF1 = [S_.mark("pe"), S_.mark("dv")]
    for ename in ("tensor", "vector", "scalar"):
        for m in joinF1:
            S_.wait(ename, m)
    pf1.close()

    pf2 = ExitStack()
    fb = [pf2.enter_context(nc.psum_tensor(f"fb{i}", [128, 512], F32))
          for i in range(8)]

    def emit_w13_load(e_, fc, slot):
        S_.wait("gpsimd", mk.get(("w13", slot)))
        for (wt, buf) in ((w1, w1b), (w3, w3b)):
            src = bass.AP(wt, e_ * D * FF + fc * 512,
                          [[FF, 128], [128 * FF, 8], [1, 512]])
            S_.dma("gpsimd", buf[slot][:, :].rearrange("p (c f) -> p c f", c=8),
                   src, "qg")
        mk[("w13_ld", slot)] = S_.mark("qg")

    def emit_w2_load(e_, fi, slot):
        S_.wait("gpsimd", mk.get(("w2", slot)))
        src = bass.AP(w2, e_ * FF * D + fi * 128 * D, [[D, 128], [1, D]])
        S_.dma("gpsimd", w2b[slot][:, :], src, "qg")
        mk[("w2_ld", slot)] = S_.mark("qg")

    emit_w13_load(0, 0, 0)
    for sup in range(2):
        S_.wait("sync", mk.get("acc"))
        S_.dma("sync", acc[:, :].rearrange("p (a f) -> p a f", a=8),
               bass.AP(y1_scr, sup * 1024 * D, [[D, 128], [128 * D, 8], [1, D]]),
               "qs")
        t_acc = S_.mark("qs")
        for e_ in range(E):
            for fc in range(8):
                slot = fc % 2
                nslot = (fc + 1) % 2
                if fc + 1 < 8:
                    emit_w13_load(e_, fc + 1, nslot)
                elif sup * E + e_ + 1 < 2 * E:
                    ne = (sup * E + e_ + 1) % E
                    emit_w13_load(ne, 0, nslot)
                for fs in range(4):
                    for tc in range(2):
                        bi = (fs * 2 + tc) % 2
                        gbank, ubank = fb[bi], fb[2 + bi]
                        S_.wait("tensor", mk[("w13_ld", slot)])
                        S_.wait("tensor", t_y1fm)
                        S_.wait("tensor", t_wi)
                        S_.wait("tensor", mk.get(("fb", bi)))
                        S_.wait("tensor", mk.get(("fb", 2 + bi)))
                        for c in range(8):
                            S_.emit("tensor", lambda e, b=gbank, c=c, sl=slot,
                                    fs=fs, tc=tc, sup=sup: e.matmul(
                                b[:, :],
                                w1b[sl][:, c * 512 + fs * 128:
                                        c * 512 + (fs + 1) * 128],
                                y1_fm[:, c * 2048 + sup * 1024 + tc * 512:
                                      c * 2048 + sup * 1024 + (tc + 1) * 512],
                                start=(c == 0), stop=(c == 7),
                                skip_group_check=True), inc="pe")
                        t_g = S_.mark("pe")
                        for c in range(8):
                            S_.emit("tensor", lambda e, b=ubank, c=c, sl=slot,
                                    fs=fs, tc=tc, sup=sup: e.matmul(
                                b[:, :],
                                w3b[sl][:, c * 512 + fs * 128:
                                        c * 512 + (fs + 1) * 128],
                                y1_fm[:, c * 2048 + sup * 1024 + tc * 512:
                                      c * 2048 + sup * 1024 + (tc + 1) * 512],
                                start=(c == 0), stop=(c == 7),
                                skip_group_check=True), inc="pe")
                        t_u = S_.mark("pe")
                        S_.wait("scalar", t_g)
                        S_.wait("scalar", mk.get(("sG", bi)))
                        S_.emit("scalar", lambda e, b=gbank, ss=bi: e.activation(
                            sG[ss][:, :], b[:, :], AF.Silu), inc="ac")
                        t_si = S_.mark("ac")
                        mk[("fb", bi)] = t_si
                        fi = fc * 4 + fs
                        S_.wait("vector", t_si)
                        S_.wait("vector", t_u)
                        S_.emit("vector", lambda e, b=ubank, ss=bi, fi=fi,
                                tc=tc: e.tensor_mul(
                            hT[:, fi * 1024 + tc * 512:fi * 1024 + (tc + 1) * 512],
                            sG[ss][:, :], b[:, :]), inc="dv")
                        mk[("fb", 2 + bi)] = S_.mark("dv")
                        mk[("sG", bi)] = S_.mark("dv")
                mk[("w13", slot)] = S_.mark("pe")
            t_hT = S_.mark("dv")
            for pfi in range(4):
                emit_w2_load(e_, pfi, pfi % 4)
            for tbh in range(2):
                for fi in range(32):
                    slot = fi % 4
                    S_.wait("tensor", mk[("w2_ld", slot)])
                    S_.wait("tensor", t_hT)
                    if fi == 0:
                        for bi in range(8):
                            S_.wait("tensor", mk.get(("fb", bi)))
                    for tb in range(4):
                        tbg = tbh * 4 + tb
                        for dh in range(2):
                            bank = fb[tb * 2 + dh]
                            S_.emit("tensor", lambda e, b=bank, fi=fi, sl=slot,
                                    tbg=tbg, dh=dh: e.matmul(
                                b[:, :],
                                hT[:, fi * 1024 + tbg * 128:
                                   fi * 1024 + (tbg + 1) * 128],
                                w2b[sl][:, dh * 512:(dh + 1) * 512],
                                start=(fi == 0), stop=(fi == 31),
                                skip_group_check=True), inc="pe")
                    mk[("w2", slot)] = S_.mark("pe")
                    if fi + 4 < 32:
                        emit_w2_load(e_, fi + 4, (fi + 4) % 4)
                    elif tbh == 0:
                        emit_w2_load(e_, (fi + 4) % 32, (fi + 4) % 4)
                t_b = S_.mark("pe")
                S_.wait("vector", t_b)
                S_.wait("vector", t_acc)
                S_.wait("vector", t_wi)
                for tb in range(4):
                    tbg = tbh * 4 + tb
                    lg = sup * 8 + tbg
                    for dh in range(2):
                        bank = fb[tb * 2 + dh]
                        S_.emit("vector", lambda e, b=bank, tbg=tbg, dh=dh,
                                lg=lg, e_=e_: e.scalar_tensor_tensor(
                            acc[:, tbg * D + dh * 512:tbg * D + (dh + 1) * 512],
                            b[:, :], wi_sb[:, lg * E + e_:lg * E + e_ + 1],
                            acc[:, tbg * D + dh * 512:tbg * D + (dh + 1) * 512],
                            ALU.mult, ALU.add), inc="dv")
                        mk[("fb", tb * 2 + dh)] = S_.mark("dv")
        for tb in range(8):
            lg = sup * 8 + tb
            sl = tb % 2
            a = acc[:, tb * D:(tb + 1) * D]
            S_.wait("vector", S_.mark("dv"))
            S_.emit("vector", lambda e, a=a: e.reduce_sum(
                ln_s[:, :], a, axis=AX.X, negate=True), inc="dv")
            S_.wait("vector", S_.mark("dv"))
            S_.emit("vector", lambda e: e.tensor_scalar(
                ln_s[:, :], ln_s[:, :], 1.0 / D, None, ALU.mult), inc="dv")
            t_m = S_.mark("dv")
            S_.wait("scalar", t_m)
            S_.emit("scalar", lambda e, a=a: e.activation(
                junkf[:, :], a, AF.Square, accum_out=ln_ss[:, :]), inc="ac")
            t_sq = S_.mark("ac")
            S_.wait("vector", t_sq)
            S_.emit("vector", lambda e: e.tensor_scalar(
                ln_t1[:, :], ln_ss[:, :], 1.0 / D, None, ALU.mult), inc="dv")
            S_.emit("vector", lambda e: e.tensor_mul(
                ln_t2[:, :], ln_s[:, :], ln_s[:, :]), inc="dv")
            S_.wait("vector", S_.mark("dv"))
            S_.emit("vector", lambda e: e.tensor_sub(
                ln_t1[:, :], ln_t1[:, :], ln_t2[:, :]), inc="dv")
            S_.wait("vector", S_.mark("dv"))
            S_.emit("vector", lambda e: e.tensor_scalar(
                ln_t1[:, :], ln_t1[:, :], EPS, None, ALU.add), inc="dv")
            t_v = S_.mark("dv")
            S_.wait("scalar", t_v)
            S_.emit("scalar", lambda e: e.activation(
                ln_r[:, :], ln_t1[:, :], AF.Ln), inc="ac")
            S_.wait("scalar", S_.mark("ac"))
            S_.emit("scalar", lambda e: e.activation(
                ln_r[:, :], ln_r[:, :], AF.Exp, scale=-0.5), inc="ac")
            t_rs = S_.mark("ac")
            S_.wait("vector", t_rs)
            S_.wait("vector", mk.get(("ystage", sl)))
            S_.emit("vector", lambda e, sl=sl, a=a: e.tensor_scalar(
                ystage[sl][:, :], a, ln_s[:, :], ln_r[:, :], ALU.add, ALU.mult),
                inc="dv")
            t_y = S_.mark("dv")
            S_.wait("sync", t_y)
            S_.dma("sync", y_out[lg * 128:(lg + 1) * 128, :], ystage[sl][:, :],
                   inc="so")
            mk[("ystage", sl)] = S_.mark("so")
        mk["acc"] = S_.mark("dv")

    S_.wait("sync", S_.mark("so"))
    pf2.close()
    pf.close()

    S_.build()
    glob.close()
    return nc


_NC_CACHE = {}


def kernel(L0_attn_x, L0_attn_qkv, L0_attn_out_w, L0_attn_rpe_bias,
           L0_ffn_gate_w, L0_ffn_w1, L0_ffn_w3, L0_ffn_w2):
    x = np.asarray(L0_attn_x, dtype=np.float32)
    qkv = np.ascontiguousarray(
        np.asarray(L0_attn_qkv, dtype=np.float32).reshape(D, 3 * D))
    outw = np.ascontiguousarray(
        np.asarray(L0_attn_out_w, dtype=np.float32).reshape(D, D))
    rpe = np.asarray(L0_attn_rpe_bias, dtype=np.float32)
    gw = np.ascontiguousarray(np.asarray(L0_ffn_gate_w, dtype=np.float32))
    w1 = np.ascontiguousarray(np.asarray(L0_ffn_w1, dtype=np.float32))
    w3 = np.ascontiguousarray(np.asarray(L0_ffn_w3, dtype=np.float32))
    w2 = np.ascontiguousarray(np.asarray(L0_ffn_w2, dtype=np.float32))
    idn = np.eye(128, dtype=np.float32)

    in_maps = []
    for c in range(8):
        b, g = c // 2, c % 2
        rpey = np.zeros((NH, VW), dtype=np.float32)
        rpey[:, 1024 - g:3072 - g] = rpe
        mkv = np.zeros((NH, VW), dtype=np.float32)
        mkv[:, :1024 - g] = -30000.0
        in_maps.append({
            "x_full": np.ascontiguousarray(x[b]),
            "x_own": np.ascontiguousarray(x[b, g::2, :]),
            "wqkv": qkv, "wout": outw, "rpey": rpey, "maskv": mkv,
            "gate": gw, "w1": w1, "w3": w3, "w2": w2, "ident": idn,
        })
    if "nc" not in _NC_CACHE:
        _NC_CACHE["nc"] = build_program()
    nc = _NC_CACHE["nc"]
    res = run_bass_kernel_spmd(nc, in_maps, core_ids=list(range(8)))
    out = np.empty((4, S, D), dtype=np.float32)
    for c in range(8):
        b, g = c // 2, c % 2
        out[b, g::2, :] = res.results[c]["y"]
    return out
